# version 04
# baseline (speedup 1.0000x reference)
import math
from contextlib import ExitStack

import numpy as np
import ml_dtypes

import concourse.bass as bass
import concourse.mybir as mybir
from concourse.bass_utils import run_bass_kernel_spmd

F32 = mybir.dt.float32
BF16 = mybir.dt.bfloat16
FP8 = mybir.dt.float8e5
I32 = mybir.dt.int32
Alu = mybir.AluOpType
Act = mybir.ActivationFunctionType

B = 8
T = 4096
DM = 1024
DI = 2048
S = 16
CH = 256
P = 128
NDT = DI // P
NMT = DM // P
NCHUNK = T // CH
SG = 2
NG = S // SG
SEGW = CH + 1
GW = SG * SEGW

LOG_EPS = float(np.float32(math.log(1e-10)))
GAMMA = float(np.float32(9765625.0 / (1 << 24)))
P_INIT = float(np.float32(1 << 34))
P_UNSCALE = float(np.float32(2.0 ** -34))
LN2 = math.log(2.0)
INV_LN2 = 1.0 / LN2

QUANT_EXACT = True
STAGES = "111"

ENGS = ("v", "s", "t", "g", "q")


class Prog:

    def __init__(self, nc):
        self.nc = nc
        self.ops = {e: [] for e in ENGS}
        self.sems = {}
        self.cnt = {}
        self.clock = {e: {} for e in ENGS}
        self.snap = {}
        self.last_w = {}
        self.readers = {}
        self.pend = {}
        self.nwait = 0
        self.ninst = 0

    NDMA = 32

    def init_sems(self, es):
        for e in ENGS:
            self.sems[e] = es.enter_context(self.nc.semaphore(f"sem_{e}"))
            self.cnt[e] = 0
        for i in range(self.NDMA):
            self.sems[f"dma{i}"] = es.enter_context(
                self.nc.semaphore(f"sem_dma{i}"))
            self.cnt[f"dma{i}"] = 0
        self.dma_rr = 0

    def _wait(self, eng, sem, cnt):
        ck = self.clock[eng]
        if ck.get(sem, 0) >= cnt:
            return
        self.ops[eng].append(("w", self.sems[sem], cnt))
        self.nwait += 1
        ck[sem] = max(ck.get(sem, 0), cnt)
        sn = self.snap.get((sem, cnt))
        if sn:
            for k, v in sn.items():
                if ck.get(k, 0) < v:
                    ck[k] = v

    def _dep_waits(self, eng, reads, writes):
        deps = {}
        for h in reads:
            lw = self.last_w.get(id(h))
            if lw and deps.get(lw[0], 0) < lw[1]:
                deps[lw[0]] = lw[1]
        for h in writes:
            lw = self.last_w.get(id(h))
            if lw and deps.get(lw[0], 0) < lw[1]:
                deps[lw[0]] = lw[1]
            for sem, c in self.readers.get(id(h), {}).items():
                if deps.get(sem, 0) < c:
                    deps[sem] = c
        for sem, c in deps.items():
            self._wait(eng, sem, c)

    def _record(self, prod, reads, writes):
        for h in writes:
            self.last_w[id(h)] = prod
            self.readers[id(h)] = {}
        for h in reads:
            rd = self.readers.setdefault(id(h), {})
            if rd.get(prod[0], 0) < prod[1]:
                rd[prod[0]] = prod[1]

    def emit(self, eng, fn, reads=(), writes=(), dma=False):
        if dma:
            key = f"dma{self.dma_rr}"
            self.dma_rr = (self.dma_rr + 1) % self.NDMA
            if self.cnt[key]:
                self._wait(eng, key, self.cnt[key])
            step = 16
        else:
            key = eng
            step = 1
        self._dep_waits(eng, reads, writes)
        self.cnt[key] += step
        newc = self.cnt[key]
        self.ops[eng].append(("i", fn, self.sems[key], newc, step))
        self.ninst += 1
        sn = dict(self.clock[eng])
        sn[key] = newc
        self.snap[(key, newc)] = sn
        self._record((key, newc), reads, writes)

    def mm(self, psum, lhsT, rhs, reads, first, last, rows=None, inc_each=False):
        if first:
            self._dep_waits("t", reads, [psum])
            self.pend[id(psum)] = [] if inc_each else list(reads)
        else:
            self._dep_waits("t", reads, ())
            if not inc_each:
                self.pend[id(psum)].extend(reads)
        pap = psum[:, :] if rows is None else psum[0:rows, :]

        def fn(t, lhsT=lhsT, rhs=rhs, first=first, last=last, pap=pap):
            return t.matmul(pap, lhsT, rhs, start=first, stop=last)
        if last or inc_each:
            self.cnt["t"] += 1
            newc = self.cnt["t"]
            self.ops["t"].append(("i", fn, self.sems["t"], newc, 1))
            self.ninst += 1
            sn = dict(self.clock["t"])
            sn["t"] = newc
            self.snap[("t", newc)] = sn
            rd = list(reads) if inc_each else self.pend.pop(id(psum))
            self._record(("t", newc), rd, [psum] if last else [])
            if last and inc_each:
                self.pend.pop(id(psum), None)
        else:
            self.ops["t"].append(("n", fn))
            self.ninst += 1

    def build(self, block):
        prog = self

        def runner(engname):
            def run(eng):
                for op in prog.ops[engname]:
                    if op[0] == "w":
                        eng.wait_ge(op[1], op[2])
                    elif op[0] == "i":
                        op[1](eng).then_inc(op[2], op[4])
                    else:
                        op[1](eng)
            return run

        block.vector(runner("v"))
        block.scalar(runner("s"))
        block.tensor(runner("t"))
        block.gpsimd(runner("g"))
        block.sync(runner("q"))


def build_program(nch=NCHUNK):
    t_total = nch * CH
    nc = bass.Bass("TRN2", target_bir_lowering=False)

    d_x = nc.dram_tensor("x_fm", [DM, t_total], F32, kind="ExternalInput")
    d_out = nc.dram_tensor("out_fm", [DM, t_total], F32, kind="ExternalOutput")
    d_ipw = nc.dram_tensor("ipwT", [DM, 2 * DI], F32, kind="ExternalInput")
    d_opw = nc.dram_tensor("opwT", [DI, DM], F32, kind="ExternalInput")
    d_xpw = nc.dram_tensor("xpwT", [DI, 33], F32, kind="ExternalInput")
    d_w1 = nc.dram_tensor("w1T", [DM, 4 * DM], F32, kind="ExternalInput")
    d_w2 = nc.dram_tensor("w2T", [4 * DM, DM], F32, kind="ExternalInput")
    d_b1 = nc.dram_tensor("b1c", [P, 32], F32, kind="ExternalInput")
    d_b2 = nc.dram_tensor("b2c", [P, 8], F32, kind="ExternalInput")
    d_alog = nc.dram_tensor("alogc", [P, S], F32, kind="ExternalInput")
    d_dtw = nc.dram_tensor("dtwc", [P, NDT], F32, kind="ExternalInput")
    d_dtb = nc.dram_tensor("dtbc", [P, NDT], F32, kind="ExternalInput")
    d_Dp = nc.dram_tensor("Dc", [P, NDT], F32, kind="ExternalInput")
    d_cvb = nc.dram_tensor("convbc", [P, NDT], F32, kind="ExternalInput")
    d_cvw = nc.dram_tensor("convwc", [P, 4 * NDT], F32, kind="ExternalInput")
    d_mask = nc.dram_tensor("mask_fm", [DM, t_total], BF16, kind="ExternalInput")
    d_init = nc.dram_tensor("initc", [P, GW], F32, kind="ExternalInput")
    d_onescol = nc.dram_tensor("onescol", [P, 1], F32, kind="ExternalInput")
    d_idb = nc.dram_tensor("idb", [P, P], BF16, kind="ExternalInput")
    d_cb = nc.dram_tensor("cbank", [P, 8], F32, kind="ExternalInput")
    s_w1 = nc.dram_tensor("scr_w1", [32, P, 8 * P], FP8, kind="Internal")
    s_w2 = nc.dram_tensor("scr_w2", [8, P, 32 * P], FP8, kind="Internal")
    s_ip = nc.dram_tensor("scr_ip", [32, P, 8 * P], BF16, kind="Internal")
    s_op = nc.dram_tensor("scr_op", [8, P, 16 * P], BF16, kind="Internal")

    es = ExitStack()
    sbt = lambda name, shape, dt=F32: es.enter_context(nc.sbuf_tensor(name, shape, dt))
    psb = lambda name, shape: es.enter_context(nc.psum_tensor(name, shape, F32))

    xt = [sbt(f"xt{i}", [P, CH]) for i in range(NMT)]
    xnt = [sbt(f"xnt{i}", [P, CH]) for i in range(NMT)]
    xnq = [sbt(f"xnq{i}", [P, CH], FP8) for i in range(NMT)]
    xnb = [sbt(f"xnb{i}", [P, CH], BF16) for i in range(NMT)]
    hq = [sbt(f"hq{i}", [P, CH], FP8) for i in range(32)]
    hbf = sbt("hbf", [P, CH], BF16)
    y2b = sbt("y2b", [P, CH], BF16)
    r1t = sbt("r1t", [P, CH])
    mub = sbt("mub", [P, CH])
    e2b = sbt("e2b", [P, CH])
    m2t = sbt("m2t", [P, CH])
    vart = sbt("vart", [P, CH])
    istd = sbt("istd", [P, CH])
    murow = sbt("murow", [1, CH])
    e2row = sbt("e2row", [1, CH])
    xit = [sbt(f"xit{i}", [P, CH + 3]) for i in range(NDT)]
    xct = [sbt(f"xct{i}", [P, CH]) for i in range(NDT)]
    xcb = [sbt(f"xcbt{i}", [P, CH], BF16) for i in range(NDT)]
    sgt = [sbt(f"sgt{i}", [P, CH]) for i in range(NDT)]
    dlt = [sbt(f"dlt{i}", [P, CH]) for i in range(2)]
    wtt = [sbt(f"wtt{i}", [P, CH]) for i in range(2)]
    spin = sbt("spin", [P, CH])
    srelu = sbt("srelu", [P, CH])
    sabs = sbt("sabs", [P, CH])
    sl1p = sbt("sl1p", [P, CH])
    xpsb = sbt("xpsb", [33, CH])
    xp0b = sbt("xp0b", [P, CH])
    ygb = [sbt(f"ygb{i}", [P, CH], BF16) for i in range(NDT)]
    y1t = [sbt(f"y1t{i}", [P, CH]) for i in range(2)]
    MB = sbt("MB", [P, GW])
    ABC = sbt("ABC", [P, GW])
    PB = sbt("PB", [P, GW])
    UB = sbt("UB", [P, GW])
    BX = sbt("BX", [P, GW])
    CBUF = sbt("CBUF", [P, GW])
    HB = sbt("HB", [P, GW])
    HC = [sbt(f"HC{i}", [P, GW]) for i in range(2)]
    HCC = [sbt(f"HCC{i}", [P, GW], BF16) for i in range(2)]
    Bb = sbt("Bb", [P, S * SEGW])
    Cb = sbt("Cb", [P, S * SEGW])
    hprev = [sbt(f"hprev{i}", [P, S]) for i in range(NDT)]
    initt = sbt("initt", [P, GW])
    w1s = [sbt(f"w1s{i}", [P, 8 * P], FP8) for i in range(2)]
    w2s = sbt("w2s0", [P, 32 * P], FP8)
    ips = [sbt(f"ips{i}", [P, 8 * P], BF16) for i in range(2)]
    opst = sbt("opst0", [P, 16 * P], BF16)
    maskt = [sbt(f"maskt{i}", [P, CH], BF16) for i in range(NMT)]
    cvf = [Bb, Cb]
    alogt = sbt("alogt", [P, S])
    negE = sbt("negE", [P, S])
    dtwt = sbt("dtwt", [P, NDT])
    dtbt = sbt("dtbt", [P, NDT])
    Dpt = sbt("Dpt", [P, NDT])
    cvbt = sbt("cvbt", [P, NDT])
    cvwt = sbt("cvwt", [P, 4 * NDT])
    b1f = sbt("b1f", [P, 32])
    b2f = sbt("b2f", [P, 8])
    b1q = sbt("b1q", [P, 32], FP8)
    b2q = sbt("b2q", [P, 8], FP8)
    onct = sbt("onct", [P, 1])
    idbt = sbt("idbt", [P, P], BF16)
    cbank = sbt("cbank_t", [P, 8])
    xpwt = sbt("xpwt", [P, NDT * 33], BF16)
    xpwf = sbt("xpwf", [P, NDT * 33])
    q1 = sbt("q1", [P, CH])
    q2 = sbt("q2", [P, CH])
    q3 = sbt("q3", [P, CH])
    q4 = sbt("q4", [P, CH])
    q5 = sbt("q5", [P, CH])
    q6 = sbt("q6", [P, CH])
    q2b = sbt("q2b", [P, CH], BF16)
    q6b = sbt("q6b", [P, CH], BF16)
    qi = sbt("qi", [P, CH], I32)

    ps_a = psb("ps_a", [P, CH])
    ps_b = psb("ps_b", [P, CH])
    ps_c = psb("ps_c", [P, CH])
    ps_d = psb("ps_d", [P, CH])
    ps_e = psb("ps_e", [P, CH])
    ps_s1 = psb("ps_s1", [1, CH])
    ps_s2 = psb("ps_s2", [1, CH])

    pr = Prog(nc)
    pr.init_sems(es)
    block = es.enter_context(nc.Block())
    E = pr.emit

    def DMA(dst_ap, src_ap, reads, writes):
        E("q", lambda q, d=dst_ap, s=src_ap: q.dma_start(d, s),
          reads=reads, writes=writes, dma=True)

    def BCAST(dst_ap, src_t, row, rowstride, width, reads, writes):
        src = bass.AP(src_t, row * rowstride, [[rowstride, 1], [0, P], [1, width]])
        E("q", lambda q, d=dst_ap, s=src: q.dma_start(d, s),
          reads=reads, writes=writes, dma=True)

    CB_LN1P, CB_EPS, CB_G38, CB_EB1, CB_EB2, CB_EB3, CB_ZERO, CB_EB4 = range(8)
    cbcol = lambda i: cbank[:, i:i + 1]

    for dst, src in ((alogt, d_alog), (dtwt, d_dtw), (dtbt, d_dtb),
                     (Dpt, d_Dp), (cvbt, d_cvb), (cvwt, d_cvw),
                     (b1f, d_b1), (b2f, d_b2), (onct, d_onescol),
                     (idbt, d_idb), (cbank, d_cb), (initt, d_init)):
        DMA(dst[:, :], src[:, :], [src], [dst])
    DMA(xpwf[:, :], bass.AP(d_xpw, 0, [[33, P], [P * 33, NDT], [1, 33]]),
        [d_xpw], [xpwf])
    E("v", lambda v: v.tensor_copy(xpwt[:, :], xpwf[:, :]), [xpwf], [xpwt])
    E("s", lambda s: s.activation(q1[:, 0:S], alogt[:, :], Act.Exp),
      [alogt], [q1])
    E("v", lambda v: v.tensor_scalar_mul(negE[:, :], q1[:, 0:S], -1.0),
      [q1], [negE])
    for h in hprev:
        E("g", lambda g, h=h: g.memset(h[:, :], 0.0), (), [h])
    for x in xit:
        E("g", lambda g, x=x: g.memset(x[:, 0:3], 0.0), (), [x])
    for btile in (MB, ABC, CBUF, HB, HC[0], HC[1], Bb, Cb, UB, BX,
                  HCC[0], HCC[1], PB, q1, q2, q3, q4, q5, q6, qi, xpsb):
        E("g", lambda g, b=btile: g.memset(b[:, :], 0.0), (), [btile])

    def ufloor(out, src_ap, srch, w):
        E("v", lambda v: v.tensor_copy(qi[:, :w], src_ap), [srch], [qi])
        E("v", lambda v: v.tensor_copy(q5[:, :w], qi[:, :w]), [qi], [q5])
        E("v", lambda v: v.tensor_tensor(out[:, :w], q5[:, :w], src_ap, Alu.is_gt),
          [q5, srch], [out])
        E("v", lambda v: v.tensor_tensor(out[:, :w], q5[:, :w], out[:, :w],
                                         Alu.subtract), [q5, out], [out])

    def fq_exact(src_ap, srch, out_ap, outh, w):
        E("s", lambda s: s.activation(q1[:, :w], src_ap, Act.Abs, bias=cbcol(CB_G38)),
          [srch, cbank], [q1])
        E("s", lambda s: s.activation(q2[:, :w], q1[:, :w], Act.Ln), [q1], [q2])
        E("v", lambda v: v.tensor_scalar(q3[:, :w], q2[:, :w], INV_LN2, 192.0,
                                         Alu.mult, Alu.add), [q2], [q3])
        ufloor(q4, q3[:, :w], q3, w)
        E("v", lambda v: v.tensor_scalar_max(q4[:, :w], q4[:, :w], 178.0),
          [q4], [q4])
        E("s", lambda s: s.activation(q2b[:, :w], q4[:, :w], Act.Exp,
                                      bias=cbcol(CB_EB4), scale=-LN2),
          [q4, cbank], [q2b])
        E("s", lambda s: s.activation(q6b[:, :w], q4[:, :w], Act.Exp,
                                      bias=cbcol(CB_EB3), scale=LN2),
          [q4, cbank], [q6b])
        E("v", lambda v: v.tensor_tensor(q1[:, :w], src_ap, q2b[:, :w], Alu.mult),
          [srch, q2b], [q1])
        E("v", lambda v: v.tensor_scalar_add(q2[:, :w], q1[:, :w], 0.5), [q1], [q2])
        ufloor(q3, q2[:, :w], q2, w)
        E("v", lambda v: v.tensor_tensor(out_ap, q3[:, :w], q6b[:, :w], Alu.mult),
          [q3, q6b], [outh])

    def rmac_exact(srch, out_ap, outh):
        w = CH
        E("s", lambda s: s.activation(q2[:, :w], srch[:, :w], Act.Abs,
                                      bias=cbcol(CB_G38)), [srch, cbank], [q2])
        E("s", lambda s: s.activation(q3[:, :w], q2[:, :w], Act.Ln), [q2], [q3])
        E("v", lambda v: v.tensor_scalar(q2[:, :w], q3[:, :w], INV_LN2, 192.0,
                                         Alu.mult, Alu.add), [q3], [q2])
        ufloor(q4, q2[:, :w], q2, w)
        E("v", lambda v: v.tensor_scalar_max(q4[:, :w], q4[:, :w], 152.0),
          [q4], [q4])
        E("s", lambda s: s.activation(q2b[:, :w], q4[:, :w], Act.Exp,
                                      bias=cbcol(CB_EB1), scale=-LN2),
          [q4, cbank], [q2b])
        E("s", lambda s: s.activation(q6b[:, :w], q4[:, :w], Act.Exp,
                                      bias=cbcol(CB_EB2), scale=LN2),
          [q4, cbank], [q6b])
        E("v", lambda v: v.tensor_tensor(q3[:, :w], srch[:, :w], q2b[:, :w],
                                         Alu.mult), [srch, q2b], [q3])
        E("v", lambda v: v.tensor_scalar_add(q2[:, :w], q3[:, :w], 0.5), [q3], [q2])
        ufloor(q3, q2[:, :w], q2, w)
        E("v", lambda v: v.tensor_tensor(out_ap, q3[:, :w], q6b[:, :w], Alu.mult),
          [q3, q6b], [outh])

    THN = [math.log2(144.5 / 128.0), math.log2(175.5 / 128.0),
           math.log2(208.5 / 128.0), math.log2(239.5 / 128.0)]
    THS = [192 + math.log2(th * 2.0 ** -16) for th in
           (0.5 + 2.0 ** -10, 1.5 - 2.0 ** -9, 2.5 + 2.0 ** -8, 3.5 - 2.0 ** -8)]

    def hchain(vsrc, hq_out, outh):
        w = CH
        E("s", lambda s: s.activation(q2[:, :w], vsrc[:, :w], Act.Abs,
                                      bias=cbcol(CB_G38)), [vsrc, cbank], [q2])
        E("s", lambda s: s.activation(q3[:, :w], q2[:, :w], Act.Ln), [q2], [q3])
        E("v", lambda v: v.tensor_scalar(q2[:, :w], q3[:, :w], INV_LN2, 192.0,
                                         Alu.mult, Alu.add), [q3], [q2])
        ufloor(q4, q2[:, :w], q2, w)
        E("v", lambda v: v.tensor_scalar_max(q4[:, :w], q4[:, :w], 178.0),
          [q4], [q4])
        E("s", lambda s: s.activation(q6b[:, :w], q4[:, :w], Act.Exp,
                                      bias=cbcol(CB_EB3), scale=LN2),
          [q4, cbank], [q6b])
        E("v", lambda v: v.tensor_tensor(q4[:, :w], q2[:, :w], q4[:, :w],
                                         Alu.subtract), [q2, q4], [q4])
        E("v", lambda v: v.tensor_single_scalar(q3[:, :w], q4[:, :w], THN[0],
                                                Alu.is_ge), [q4], [q3])
        for th in THN[1:]:
            E("v", lambda v, th=th: v.scalar_tensor_tensor(
                q3[:, :w], q4[:, :w], th, q3[:, :w], Alu.is_ge, Alu.add),
              [q4, q3], [q3])
        for th in THS:
            E("v", lambda v, th=th: v.scalar_tensor_tensor(
                q3[:, :w], q2[:, :w], th, q3[:, :w], Alu.is_ge, Alu.add),
              [q2, q3], [q3])
        E("v", lambda v: v.tensor_tensor(q2[:, :w], q3[:, :w], q6b[:, :w],
                                         Alu.mult), [q3, q6b], [q2])
        E("v", lambda v: v.tensor_single_scalar(q3[:, :w], vsrc[:, :w], 0.0,
                                                Alu.is_gt), [vsrc], [q3])
        E("v", lambda v: v.tensor_tensor(hq_out, q2[:, :w], q3[:, :w], Alu.mult),
          [q2, q3], [outh])

    W8 = 8 * P
    for n in range(32):
        buf = cvf[n % 2]
        DMA(buf[:, 0:W8], bass.AP(d_w1, n * P, [[4 * DM, P], [P * 4 * DM, 8], [1, P]]),
            [d_w1], [buf])
        if QUANT_EXACT:
            for piece in range(4):
                sl = slice(piece * CH, (piece + 1) * CH)
                fq_exact(buf[:, sl], buf, w1s[0][:, sl], w1s[0], CH)
        else:
            E("v", lambda v, b=buf: v.tensor_copy(w1s[0][:, :], b[:, 0:W8]),
              [buf], [w1s[0]])
        DMA(s_w1[n, :, :], w1s[0][:, :], [w1s[0]], [s_w1])
    for mt in range(8):
        for half in range(4):
            buf = cvf[half % 2]
            src = bass.AP(d_w2, mt * P + half * 8 * P * DM,
                          [[DM, P], [P * DM, 8], [1, P]])
            DMA(buf[:, 0:W8], src, [d_w2], [buf])
            if QUANT_EXACT:
                for piece in range(4):
                    sl = slice(piece * CH, (piece + 1) * CH)
                    osl = slice(half * 8 * P + piece * CH,
                                half * 8 * P + (piece + 1) * CH)
                    fq_exact(buf[:, sl], buf, w2s[:, osl], w2s, CH)
            else:
                E("v", lambda v, b=buf, h=half: v.tensor_copy(
                    w2s[:, h * 8 * P:(h + 1) * 8 * P], b[:, 0:W8]), [buf], [w2s])
        DMA(s_w2[mt, :, :], w2s[:, :], [w2s], [s_w2])
    for n in range(32):
        buf = cvf[n % 2]
        DMA(buf[:, 0:W8], bass.AP(d_ipw, n * P, [[2 * DI, P], [P * 2 * DI, 8], [1, P]]),
            [d_ipw], [buf])
        E("v", lambda v, b=buf: v.tensor_copy(ips[0][:, :], b[:, 0:W8]), [buf], [ips[0]])
        DMA(s_ip[n, :, :], ips[0][:, :], [ips[0]], [s_ip])
    for mt in range(8):
        for half in range(2):
            buf = cvf[half]
            src = bass.AP(d_opw, mt * P + half * 8 * P * DM,
                          [[DM, P], [P * DM, 8], [1, P]])
            DMA(buf[:, 0:W8], src, [d_opw], [buf])
            E("v", lambda v, b=buf, h=half: v.tensor_copy(
                opst[:, h * 8 * P:(h + 1) * 8 * P], b[:, 0:W8]), [buf], [opst])
        DMA(s_op[mt, :, :], opst[:, :], [opst], [s_op])
    if QUANT_EXACT:
        fq_exact(b1f[:, 0:32], b1f, q4[:, 0:32], q4, 32)
        E("v", lambda v: v.tensor_copy(b1f[:, 0:32], q4[:, 0:32]), [q4], [b1f])
        fq_exact(b2f[:, 0:8], b2f, q4[:, 0:8], q4, 8)
        E("v", lambda v: v.tensor_copy(b2f[:, 0:8], q4[:, 0:8]), [q4], [b2f])
    else:
        E("v", lambda v: v.tensor_copy(b1q[:, :], b1f[:, :]), [b1f], [b1q])
        E("v", lambda v: v.tensor_copy(b1f[:, :], b1q[:, :]), [b1q], [b1f])
        E("v", lambda v: v.tensor_copy(b2q[:, :], b2f[:, :]), [b2f], [b2q])
        E("v", lambda v: v.tensor_copy(b2f[:, :], b2q[:, :]), [b2q], [b2f])

    def layernorm():
        for mt in range(NMT):
            E("s", lambda s, mt=mt: s.activation(xnt[mt][:, :], xt[mt][:, :],
                                                 Act.Square), [xt[mt]], [xnt[mt]])
        for mt in range(NMT):
            pr.mm(ps_s1, onct[:, 0:1], xt[mt][:, :], [onct, xt[mt]],
                  mt == 0, mt == NMT - 1)
        for mt in range(NMT):
            pr.mm(ps_s2, onct[:, 0:1], xnt[mt][:, :], [onct, xnt[mt]],
                  mt == 0, mt == NMT - 1)
        E("s", lambda s: s.activation(murow[0:1, :], ps_s1[0:1, :], Act.Copy,
                                      scale=1.0 / DM), [ps_s1], [murow])
        E("s", lambda s: s.activation(e2row[0:1, :], ps_s2[0:1, :], Act.Copy,
                                      scale=1.0 / DM), [ps_s2], [e2row])
        BCAST(mub[:, :], murow, 0, CH, CH, [murow], [mub])
        BCAST(e2b[:, :], e2row, 0, CH, CH, [e2row], [e2b])
        E("s", lambda s: s.activation(m2t[:, :], mub[:, :], Act.Square),
          [mub], [m2t])
        E("v", lambda v: v.tensor_tensor(vart[:, :], e2b[:, :], m2t[:, :],
                                         Alu.subtract), [e2b, m2t], [vart])
        E("s", lambda s: s.activation(vart[:, :], vart[:, :], Act.Ln,
                                      bias=cbcol(CB_EPS)), [vart, cbank], [vart])
        E("s", lambda s: s.activation(istd[:, :], vart[:, :], Act.Exp, scale=-0.5),
          [vart], [istd])
        for mt in range(NMT):
            E("v", lambda v, mt=mt: v.tensor_tensor(xnt[mt][:, :], xt[mt][:, :],
                                                    mub[:, :], Alu.subtract),
              [xt[mt], mub], [xnt[mt]])
            E("v", lambda v, mt=mt: v.tensor_tensor(xnt[mt][:, :], xnt[mt][:, :],
                                                    istd[:, :], Alu.mult),
              [xnt[mt], istd], [xnt[mt]])

    def ffn():
        for mt in range(NMT):
            if QUANT_EXACT:
                fq_exact(xnt[mt][:, :], xnt[mt], xnq[mt][:, :], xnq[mt], CH)
            else:
                E("v", lambda v, mt=mt: v.tensor_copy(xnq[mt][:, :], xnt[mt][:, :]),
                  [xnt[mt]], [xnq[mt]])
        for n in range(32):
            strip = w1s[n % 2]
            DMA(strip[:, :], s_w1[n, :, :], [s_w1], [strip])
            ps = ps_a if n % 2 == 0 else ps_b
            for k in range(8):
                pr.mm(ps, strip[:, k * P:(k + 1) * P], xnq[k][:, :],
                      [strip, xnq[k]], k == 0, k == 7)
            if QUANT_EXACT:
                E("s", lambda s, ps=ps, n=n: s.activation(
                    q1[:, :], ps[:, :], Act.Identity, bias=b1f[:, n:n + 1]),
                  [ps, b1f], [q1])
                hchain(q1, hq[n][:, :], hq[n])
            else:
                E("s", lambda s, ps=ps, n=n: s.activation(
                    hbf[:, :], ps[:, :], Act.Identity, bias=b1f[:, n:n + 1]),
                  [ps, b1f], [hbf])
                E("v", lambda v, n=n: v.tensor_scalar_max(hq[n][:, :], hbf[:, :],
                                                          0.0), [hbf], [hq[n]])
        for mt in range(NMT):
            DMA(w2s[:, :], s_w2[mt, :, :], [s_w2], [w2s])
            ps = ps_c if mt % 2 == 0 else ps_d
            for n in range(32):
                pr.mm(ps, w2s[:, n * P:(n + 1) * P], hq[n][:, :],
                      [w2s, hq[n]], n == 0, n == 31)
            if QUANT_EXACT:
                E("s", lambda s, ps=ps, mt=mt: s.activation(
                    q1[:, :], ps[:, :], Act.Identity, bias=b2f[:, mt:mt + 1]),
                  [ps, b2f], [q1])
                rmac_exact(q1, y2b[:, :], y2b)
            else:
                E("s", lambda s, ps=ps, mt=mt: s.activation(
                    y2b[:, :], ps[:, :], Act.Identity, bias=b2f[:, mt:mt + 1]),
                  [ps, b2f], [y2b])
            E("v", lambda v, mt=mt: v.tensor_tensor(r1t[:, :], y2b[:, :],
                                                    maskt[mt][:, :], Alu.mult),
              [y2b, maskt[mt]], [r1t])
            E("v", lambda v, mt=mt: v.tensor_tensor(xt[mt][:, :], xt[mt][:, :],
                                                    r1t[:, :], Alu.add),
              [xt[mt], r1t], [xt[mt]])

    def mamba(c):
        for mt in range(NMT):
            E("v", lambda v, mt=mt: v.tensor_copy(xnb[mt][:, :], xnt[mt][:, :]),
              [xnt[mt]], [xnb[mt]])
        for n in range(32):
            strip = ips[n % 2]
            DMA(strip[:, :], s_ip[n, :, :], [s_ip], [strip])
            ps = ps_a if n % 2 == 0 else ps_b
            for k in range(8):
                pr.mm(ps, strip[:, k * P:(k + 1) * P], xnb[k][:, :],
                      [strip, xnb[k]], k == 0, k == 7)
            if n < NDT:
                if c > 0:
                    E("v", lambda v, n=n: v.tensor_copy(xit[n][:, 0:3],
                                                        xit[n][:, CH:CH + 3]),
                      [xit[n]], [xit[n]])
                E("s", lambda s, ps=ps, n=n: s.copy(xit[n][:, 3:CH + 3], ps[:, :]),
                  [ps], [xit[n]])
            else:
                dt = n - NDT
                E("s", lambda s, ps=ps, dt=dt: s.activation(sgt[dt][:, :],
                                                            ps[:, :], Act.Silu),
                  [ps], [sgt[dt]])
        for dt in range(NDT):
            E("v", lambda v, dt=dt: v.tensor_scalar_mul(
                spin[:, :], xit[dt][:, 0:CH], cvwt[:, dt:dt + 1]),
              [xit[dt], cvwt], [spin])
            for j in range(1, 4):
                E("v", lambda v, dt=dt, j=j: v.scalar_tensor_tensor(
                    spin[:, :], xit[dt][:, j:j + CH],
                    cvwt[:, j * NDT + dt:j * NDT + dt + 1],
                    spin[:, :], Alu.mult, Alu.add),
                  [xit[dt], cvwt, spin], [spin])
            E("s", lambda s, dt=dt: s.activation(xct[dt][:, :], spin[:, :],
                                                 Act.Silu, bias=cvbt[:, dt:dt + 1]),
              [spin, cvbt], [xct[dt]])
            E("v", lambda v, dt=dt: v.tensor_copy(xcb[dt][:, :], xct[dt][:, :]),
              [xct[dt]], [xcb[dt]])
        for dt in range(NDT):
            pr.mm(ps_e, xpwt[:, dt * 33:(dt + 1) * 33], xcb[dt][:, :],
                  [xpwt, xcb[dt]], dt == 0, dt == NDT - 1, rows=33)
        E("s", lambda s: s.copy(xpsb[0:33, :], ps_e[0:33, :]), [ps_e], [xpsb])
        BCAST(xp0b[:, :], xpsb, 0, CH, CH, [xpsb], [xp0b])
        for srow in range(S):
            BCAST(Bb[:, srow * SEGW + 1:srow * SEGW + 1 + CH], xpsb, 1 + srow,
                  CH, CH, [xpsb], [Bb])
            BCAST(Cb[:, srow * SEGW + 1:srow * SEGW + 1 + CH], xpsb, 17 + srow,
                  CH, CH, [xpsb], [Cb])
        for dt in range(NDT):
            dl = dlt[dt % 2]
            wt = wtt[dt % 2]
            E("v", lambda v, dt=dt, dl=dl: v.tensor_scalar(
                spin[:, :], xp0b[:, :], dtwt[:, dt:dt + 1], dtbt[:, dt:dt + 1],
                Alu.mult, Alu.add), [xp0b, dtwt, dtbt], [spin])
            E("v", lambda v: v.tensor_scalar_max(srelu[:, :], spin[:, :], 0.0),
              [spin], [srelu])
            E("s", lambda s: s.activation(sabs[:, :], spin[:, :], Act.Abs),
              [spin], [sabs])
            E("s", lambda s: s.activation(sabs[:, :], sabs[:, :], Act.Exp,
                                          scale=-1.0), [sabs], [sabs])
            E("s", lambda s: s.activation(sl1p[:, :], sabs[:, :], Act.Ln,
                                          bias=cbcol(CB_LN1P)),
              [sabs, cbank], [sl1p])
            E("v", lambda v, dl=dl: v.tensor_tensor(dl[:, :], srelu[:, :],
                                                    sl1p[:, :], Alu.add),
              [srelu, sl1p], [dl])
            E("v", lambda v, dt=dt, dl=dl, wt=wt: v.tensor_tensor(
                wt[:, :], dl[:, :], xct[dt][:, :], Alu.mult),
              [dl, xct[dt]], [wt])
            ps_y = ps_c if dt % 2 == 0 else ps_d
            nmm = 0
            for g in range(NG):
                hcb = HC[g % 2]
                hccb = HCC[g % 2]
                goff = g * SG * SEGW
                for si in range(SG):
                    srow = g * SG + si
                    E("g", lambda v, dl=dl, si=si, srow=srow: v.tensor_scalar(
                        MB[:, si * SEGW + 1:si * SEGW + 1 + CH], dl[:, :],
                        negE[:, srow:srow + 1], LOG_EPS, Alu.mult, Alu.max),
                      [dl, negE], [MB])
                mview = bass.AP(MB, 1, [[GW, P], [SEGW, SG], [1, CH]])
                aview = bass.AP(ABC, 1, [[GW, P], [SEGW, SG], [1, CH]])
                E("s", lambda s, mview=mview, aview=aview: s.activation(
                    aview, mview, Act.Exp), [MB], [ABC])
                E("v", lambda v: v.tensor_tensor_scan(
                    PB[:, 0:GW], ABC[:, 0:GW], initt[:, 0:GW], 0.0,
                    Alu.mult, Alu.add), [ABC, initt], [PB])
                E("v", lambda v: v.tensor_scalar(
                    bass.AP(UB, 1, [[GW, P], [SEGW, SG], [1, CH]]),
                    bass.AP(PB, 0, [[GW, P], [SEGW, SG], [1, CH]]),
                    GAMMA, 1.0, Alu.mult, Alu.min), [PB], [UB])
                E("g", lambda v, wt=wt, goff=goff: v.tensor_tensor(
                    bass.AP(BX, 1, [[GW, P], [SEGW, SG], [1, CH]]),
                    bass.AP(wt, 0, [[CH, P], [0, SG], [1, CH]]),
                    bass.AP(Bb, goff + 1, [[S * SEGW, P], [SEGW, SG], [1, CH]]),
                    Alu.mult), [wt, Bb], [BX])
                E("v", lambda v: v.tensor_tensor(
                    bass.AP(CBUF, 1, [[GW, P], [SEGW, SG], [1, CH]]),
                    bass.AP(BX, 1, [[GW, P], [SEGW, SG], [1, CH]]),
                    bass.AP(UB, 1, [[GW, P], [SEGW, SG], [1, CH]]),
                    Alu.mult), [BX, UB], [CBUF])
                E("v", lambda v: v.tensor_tensor_scan(
                    HB[:, 1:GW], ABC[:, 0:GW - 1], CBUF[:, 1:GW], 0.0,
                    Alu.mult, Alu.add), [ABC, CBUF], [HB])
                for si in range(SG):
                    srow = g * SG + si
                    E("v", lambda v, dt=dt, si=si, srow=srow, hcb=hcb:
                      v.scalar_tensor_tensor(
                          hcb[:, si * SEGW + 1:si * SEGW + 1 + CH],
                          PB[:, si * SEGW + 1:si * SEGW + 1 + CH],
                          hprev[dt][:, srow:srow + 1],
                          HB[:, si * SEGW + 1:si * SEGW + 1 + CH],
                          Alu.mult, Alu.add),
                      [PB, hprev[dt], HB], [hcb])
                E("g", lambda v, dt=dt, g=g, hcb=hcb: v.tensor_scalar_mul(
                    hprev[dt][:, g * SG:(g + 1) * SG],
                    bass.AP(hcb, CH, [[GW, P], [SEGW, SG]]), P_UNSCALE),
                  [hcb], [hprev[dt]])
                E("g", lambda v, hcb=hcb, hccb=hccb, goff=goff: v.tensor_tensor(
                    bass.AP(hccb, 1, [[GW, P], [SEGW, SG], [1, CH]]),
                    bass.AP(hcb, 1, [[GW, P], [SEGW, SG], [1, CH]]),
                    bass.AP(Cb, goff + 1, [[S * SEGW, P], [SEGW, SG], [1, CH]]),
                    Alu.mult), [hcb, Cb], [hccb])
                for si in range(SG):
                    pr.mm(ps_y, idbt[:, :], hccb[:, si * SEGW + 1:si * SEGW + 1 + CH],
                          [idbt, hccb], nmm == 0, nmm == S - 1, inc_each=True)
                    nmm += 1
            y1 = y1t[dt % 2]
            E("v", lambda v, dt=dt, ps_y=ps_y, y1=y1: v.scalar_tensor_tensor(
                y1[:, :], xct[dt][:, :], Dpt[:, dt:dt + 1], ps_y[:, :],
                Alu.mult, Alu.add), [xct[dt], Dpt, ps_y], [y1])
            E("v", lambda v, dt=dt, y1=y1: v.tensor_tensor(
                ygb[dt][:, :], y1[:, :], sgt[dt][:, :], Alu.mult),
              [y1, sgt[dt]], [ygb[dt]])
        for mt in range(NMT):
            DMA(opst[:, :], s_op[mt, :, :], [s_op], [opst])
            ps = ps_a if mt % 2 == 0 else ps_b
            for dt in range(NDT):
                pr.mm(ps, opst[:, dt * P:(dt + 1) * P], ygb[dt][:, :],
                      [opst, ygb[dt]], dt == 0, dt == NDT - 1)
            E("v", lambda v, mt=mt, ps=ps: v.tensor_tensor(
                xt[mt][:, :], xt[mt][:, :], ps[:, :], Alu.add),
              [xt[mt], ps], [xt[mt]])

    for c in range(nch):
        for mt in range(NMT):
            DMA(xt[mt][:, :], d_x[mt * P:(mt + 1) * P, c * CH:(c + 1) * CH],
                [d_x], [xt[mt]])
            DMA(maskt[mt][:, :], d_mask[mt * P:(mt + 1) * P, c * CH:(c + 1) * CH],
                [d_mask], [maskt[mt]])
        if STAGES[0] == "1":
            layernorm()
            ffn()
        if STAGES[1] == "1":
            layernorm()
            mamba(c)
        if STAGES[2] == "1":
            layernorm()
            ffn()
        for mt in range(NMT):
            DMA(d_out[mt * P:(mt + 1) * P, c * CH:(c + 1) * CH], xt[mt][:, :],
                [xt[mt]], [d_out])

    for i in range(pr.NDMA):
        key = f"dma{i}"
        if pr.cnt[key]:
            pr.ops["q"].append(("w", pr.sems[key], pr.cnt[key]))
    pr.build(block)
    es.close()
    return nc, pr.ninst, pr.nwait


_MASK = None


def _threefry2x32(k0, k1, x0, x1):
    R = [[13, 15, 26, 6], [17, 29, 16, 24]]

    def rotl(x, d):
        return ((x << np.uint32(d)) | (x >> np.uint32(32 - d))).astype(np.uint32)

    ks0 = np.uint32(k0)
    ks1 = np.uint32(k1)
    ks2 = np.uint32(0x1BD11BDA) ^ ks0 ^ ks1
    x0 = (x0 + ks0).astype(np.uint32)
    x1 = (x1 + ks1).astype(np.uint32)
    ks = [ks1, ks2, ks0]
    for r in range(5):
        rs = R[r % 2]
        for i in range(4):
            x0 = (x0 + x1).astype(np.uint32)
            x1 = rotl(x1, rs[i]) ^ x0
        x0 = (x0 + ks[r % 3]).astype(np.uint32)
        x1 = (x1 + ks[(r + 1) % 3] + np.uint32(r + 1)).astype(np.uint32)
    return x0, x1


def _get_mask():
    global _MASK
    if _MASK is None:
        n = B * T * DM
        idx = np.arange(n, dtype=np.uint32)
        a, bb = _threefry2x32(0, 42, np.zeros(n, np.uint32), idx)
        bits = a ^ bb
        u = ((bits >> np.uint32(9)) | np.uint32(0x3F800000)).view(np.float32) - 1.0
        keep = (u < np.float32(0.8)).reshape(B, T, DM)
        _MASK = np.where(keep, np.float32(0.625), np.float32(0.0))
    return _MASK


def _vec2col(v, n):
    return np.ascontiguousarray(np.asarray(v, np.float32).reshape(n, P).T)


def make_inputs(L0_attn_x, L0_attn_in_proj_w, L0_attn_conv_w, L0_attn_conv_b,
                L0_attn_x_proj_w, L0_attn_dt_w, L0_attn_dt_b, L0_attn_A_log,
                L0_attn_D_param, L0_attn_out_proj_w, L0_ffn_W1, L0_ffn_b1,
                L0_ffn_W2, L0_ffn_b2, mask=None, nb=B, tlen=T):
    x = np.asarray(L0_attn_x, np.float32)
    shared = dict(
        ipwT=np.ascontiguousarray(np.asarray(L0_attn_in_proj_w, np.float32).T),
        opwT=np.ascontiguousarray(np.asarray(L0_attn_out_proj_w, np.float32).T),
        xpwT=np.ascontiguousarray(np.asarray(L0_attn_x_proj_w, np.float32).T),
        w1T=np.ascontiguousarray(np.asarray(L0_ffn_W1, np.float32).T),
        w2T=np.ascontiguousarray(np.asarray(L0_ffn_W2, np.float32).T),
        b1c=_vec2col(L0_ffn_b1, 32),
        b2c=_vec2col(L0_ffn_b2, 8),
        alogc=np.ascontiguousarray(np.tile(np.asarray(L0_attn_A_log, np.float32),
                                           (P, 1))),
        dtwc=_vec2col(np.asarray(L0_attn_dt_w, np.float32)[:, 0], NDT),
        dtbc=_vec2col(L0_attn_dt_b, NDT),
        Dc=_vec2col(L0_attn_D_param, NDT),
        convbc=_vec2col(L0_attn_conv_b, NDT),
    )
    cw = np.asarray(L0_attn_conv_w, np.float32)[:, 0, :]
    shared["convwc"] = np.ascontiguousarray(
        np.concatenate([_vec2col(cw[:, j], NDT) for j in range(4)], axis=1))
    initc = np.zeros((P, GW), np.float32)
    for si in range(SG):
        initc[:, si * SEGW] = P_INIT
    shared["initc"] = initc
    shared["onescol"] = np.ones((P, 1), np.float32)
    shared["idb"] = np.ascontiguousarray(np.eye(P, dtype=ml_dtypes.bfloat16))
    cb = np.zeros((P, 8), np.float32)
    cb[:, 0] = 1.0
    cb[:, 1] = 1e-5
    cb[:, 2] = 1e-38
    cb[:, 3] = np.float32(199 * LN2)
    cb[:, 4] = np.float32(-199 * LN2)
    cb[:, 5] = np.float32(-194 * LN2)
    cb[:, 7] = np.float32(194 * LN2)
    shared["cbank"] = cb
    if mask is None:
        mask = _get_mask()
    in_maps = []
    for b in range(nb):
        m = dict(shared)
        m["x_fm"] = np.ascontiguousarray(x[b].T)
        m["mask_fm"] = np.ascontiguousarray(
            np.asarray(mask[b], np.float32).T.astype(ml_dtypes.bfloat16))
        in_maps.append(m)
    return in_maps


def kernel(**inputs):
    in_maps = make_inputs(**inputs)
    nc, _, _ = build_program(NCHUNK)
    res = run_bass_kernel_spmd(nc, in_maps, core_ids=list(range(B)))
    out = np.stack([np.asarray(r["out_fm"], np.float32).T for r in res.results])
    return np.ascontiguousarray(out)


# version 05
# speedup vs baseline: 1.5847x; 1.5847x over previous
import math
from contextlib import ExitStack

import numpy as np
import ml_dtypes

import concourse.bass as bass
import concourse.mybir as mybir
from concourse.bass_utils import run_bass_kernel_spmd

F32 = mybir.dt.float32
BF16 = mybir.dt.bfloat16
FP8 = mybir.dt.float8e5
I32 = mybir.dt.int32
Alu = mybir.AluOpType
Act = mybir.ActivationFunctionType

B = 8
T = 4096
DM = 1024
DI = 2048
S = 16
CH = 256
P = 128
NDT = DI // P
NMT = DM // P
NCHUNK = T // CH
SG = 2
NG = S // SG
SEGW = CH + 1
GW = SG * SEGW

LOG_EPS = float(np.float32(math.log(1e-10)))
GAMMA = float(np.float32(9765625.0 / (1 << 24)))
P_INIT = float(np.float32(1 << 34))
P_UNSCALE = float(np.float32(2.0 ** -34))
LN2 = math.log(2.0)
INV_LN2 = 1.0 / LN2

QUANT_EXACT = True
STAGES = "111"

ENGS = ("v", "s", "t", "g", "q")


class Prog:

    def __init__(self, nc):
        self.nc = nc
        self.ops = {e: [] for e in ENGS}
        self.sems = {}
        self.cnt = {}
        self.clock = {e: {} for e in ENGS}
        self.snap = {}
        self.last_w = {}
        self.readers = {}
        self.pend = {}
        self.nwait = 0
        self.ninst = 0

    NDMA = 32

    def init_sems(self, es):
        for e in ENGS:
            self.sems[e] = es.enter_context(self.nc.semaphore(f"sem_{e}"))
            self.cnt[e] = 0
        for i in range(self.NDMA):
            self.sems[f"dma{i}"] = es.enter_context(
                self.nc.semaphore(f"sem_dma{i}"))
            self.cnt[f"dma{i}"] = 0
        self.dma_rr = 0

    def _wait(self, eng, sem, cnt):
        ck = self.clock[eng]
        if ck.get(sem, 0) >= cnt:
            return
        self.ops[eng].append(("w", self.sems[sem], cnt))
        self.nwait += 1
        ck[sem] = max(ck.get(sem, 0), cnt)
        sn = self.snap.get((sem, cnt))
        if sn:
            for k, v in sn.items():
                if ck.get(k, 0) < v:
                    ck[k] = v

    def _dep_waits(self, eng, reads, writes):
        deps = {}
        for h in reads:
            lw = self.last_w.get(id(h))
            if lw and deps.get(lw[0], 0) < lw[1]:
                deps[lw[0]] = lw[1]
        for h in writes:
            lw = self.last_w.get(id(h))
            if lw and deps.get(lw[0], 0) < lw[1]:
                deps[lw[0]] = lw[1]
            for sem, c in self.readers.get(id(h), {}).items():
                if deps.get(sem, 0) < c:
                    deps[sem] = c
        for sem, c in deps.items():
            self._wait(eng, sem, c)

    def _record(self, prod, reads, writes):
        for h in writes:
            self.last_w[id(h)] = prod
            self.readers[id(h)] = {}
        for h in reads:
            rd = self.readers.setdefault(id(h), {})
            if rd.get(prod[0], 0) < prod[1]:
                rd[prod[0]] = prod[1]

    def emit(self, eng, fn, reads=(), writes=(), dma=False):
        if dma:
            key = f"dma{self.dma_rr}"
            self.dma_rr = (self.dma_rr + 1) % self.NDMA
            if self.cnt[key]:
                self._wait(eng, key, self.cnt[key])
            step = 16
        else:
            key = eng
            step = 1
        self._dep_waits(eng, reads, writes)
        self.cnt[key] += step
        newc = self.cnt[key]
        self.ops[eng].append(("i", fn, self.sems[key], newc, step))
        self.ninst += 1
        sn = dict(self.clock[eng])
        sn[key] = newc
        self.snap[(key, newc)] = sn
        self._record((key, newc), reads, writes)

    def mm(self, psum, lhsT, rhs, reads, first, last, rows=None, inc_each=False):
        if first:
            self._dep_waits("t", reads, [psum])
            self.pend[id(psum)] = [] if inc_each else list(reads)
        else:
            self._dep_waits("t", reads, ())
            if not inc_each:
                self.pend[id(psum)].extend(reads)
        pap = psum[:, :] if rows is None else psum[0:rows, :]

        def fn(t, lhsT=lhsT, rhs=rhs, first=first, last=last, pap=pap):
            return t.matmul(pap, lhsT, rhs, start=first, stop=last)
        if last or inc_each:
            self.cnt["t"] += 1
            newc = self.cnt["t"]
            self.ops["t"].append(("i", fn, self.sems["t"], newc, 1))
            self.ninst += 1
            sn = dict(self.clock["t"])
            sn["t"] = newc
            self.snap[("t", newc)] = sn
            rd = list(reads) if inc_each else self.pend.pop(id(psum))
            self._record(("t", newc), rd, [psum] if last else [])
            if last and inc_each:
                self.pend.pop(id(psum), None)
        else:
            self.ops["t"].append(("n", fn))
            self.ninst += 1

    def build(self, block):
        prog = self

        def runner(engname):
            def run(eng):
                for op in prog.ops[engname]:
                    if op[0] == "w":
                        eng.wait_ge(op[1], op[2])
                    elif op[0] == "i":
                        op[1](eng).then_inc(op[2], op[4])
                    else:
                        op[1](eng)
            return run

        block.vector(runner("v"))
        block.scalar(runner("s"))
        block.tensor(runner("t"))
        block.gpsimd(runner("g"))
        block.sync(runner("q"))


def build_program(nch=NCHUNK):
    t_total = nch * CH
    nc = bass.Bass("TRN2", target_bir_lowering=False)

    d_x = nc.dram_tensor("x_fm", [DM, t_total], F32, kind="ExternalInput")
    d_out = nc.dram_tensor("out_fm", [DM, t_total], F32, kind="ExternalOutput")
    d_ipw = nc.dram_tensor("ipwT", [DM, 2 * DI], F32, kind="ExternalInput")
    d_opw = nc.dram_tensor("opwT", [DI, DM], F32, kind="ExternalInput")
    d_xpw = nc.dram_tensor("xpwT", [DI, 33], F32, kind="ExternalInput")
    d_w1 = nc.dram_tensor("w1T", [DM, 4 * DM], F32, kind="ExternalInput")
    d_w2 = nc.dram_tensor("w2T", [4 * DM, DM], F32, kind="ExternalInput")
    d_b1 = nc.dram_tensor("b1c", [P, 32], F32, kind="ExternalInput")
    d_b2 = nc.dram_tensor("b2c", [P, 8], F32, kind="ExternalInput")
    d_alog = nc.dram_tensor("alogc", [P, S], F32, kind="ExternalInput")
    d_dtw = nc.dram_tensor("dtwc", [P, NDT], F32, kind="ExternalInput")
    d_dtb = nc.dram_tensor("dtbc", [P, NDT], F32, kind="ExternalInput")
    d_Dp = nc.dram_tensor("Dc", [P, NDT], F32, kind="ExternalInput")
    d_cvb = nc.dram_tensor("convbc", [P, NDT], F32, kind="ExternalInput")
    d_cvw = nc.dram_tensor("convwc", [P, 4 * NDT], F32, kind="ExternalInput")
    d_mask = nc.dram_tensor("mask_fm", [DM, t_total], BF16, kind="ExternalInput")
    d_init = nc.dram_tensor("initc", [P, GW], F32, kind="ExternalInput")
    d_onescol = nc.dram_tensor("onescol", [P, 1], F32, kind="ExternalInput")
    d_idb = nc.dram_tensor("idb", [P, P], BF16, kind="ExternalInput")
    d_cb = nc.dram_tensor("cbank", [P, 8], F32, kind="ExternalInput")
    s_w1 = nc.dram_tensor("scr_w1", [32, P, 8 * P], FP8, kind="Internal")
    s_w2 = nc.dram_tensor("scr_w2", [8, P, 32 * P], FP8, kind="Internal")
    s_ip = nc.dram_tensor("scr_ip", [32, P, 8 * P], BF16, kind="Internal")
    s_op = nc.dram_tensor("scr_op", [8, P, 16 * P], BF16, kind="Internal")

    es = ExitStack()
    sbt = lambda name, shape, dt=F32: es.enter_context(nc.sbuf_tensor(name, shape, dt))
    psb = lambda name, shape: es.enter_context(nc.psum_tensor(name, shape, F32))

    W4 = 4 * CH
    xt = [sbt(f"xt{i}", [P, W4]) for i in range(2)]
    xnt = [sbt(f"xnt{i}", [P, W4]) for i in range(2)]
    xnq = [sbt(f"xnq{i}", [P, W4], FP8) for i in range(2)]
    xnb = [sbt(f"xnb{i}", [P, W4], BF16) for i in range(2)]
    hq = [sbt(f"hq{i}", [P, W4], FP8) for i in range(8)]
    hbf = sbt("hbf", [P, CH], BF16)
    y2b = sbt("y2b", [P, W4], BF16)
    r1t = sbt("r1t", [P, W4])
    murow = sbt("murow", [1, CH])
    e2row = sbt("e2row", [1, CH])
    xit = [sbt(f"xit{i}", [P, CH + 3]) for i in range(NDT)]
    xct = [sbt(f"xct{i}", [P, CH]) for i in range(NDT)]
    xcb = [sbt(f"xcbt{i}", [P, CH], BF16) for i in range(2)]
    sgt = [sbt(f"sgt{i}", [P, CH]) for i in range(NDT)]
    dlt = [sbt(f"dlt{i}", [P, CH]) for i in range(2)]
    wtt = [sbt(f"wtt{i}", [P, CH]) for i in range(2)]
    spin = sbt("spin", [P, CH])
    srelu = sbt("srelu", [P, CH])
    sabs = sbt("sabs", [P, CH])
    sl1p = sbt("sl1p", [P, CH])
    xpsb = sbt("xpsb", [33, CH])
    xp0b = sbt("xp0b", [P, CH])
    ygb = [sbt(f"ygb{i}", [P, CH], BF16) for i in range(NDT)]
    y1t = [sbt(f"y1t{i}", [P, CH]) for i in range(2)]

    Bb = sbt("Bb", [P, S * SEGW])
    Cb = sbt("Cb", [P, S * SEGW])
    hprev = [sbt(f"hprev{i}", [P, S]) for i in range(NDT)]
    initt = sbt("initt", [P, GW])
    w1s = [sbt(f"w1s{i}", [P, 8 * P], FP8) for i in range(2)]
    w2s = sbt("w2s0", [P, 32 * P], FP8)
    ips = [sbt(f"ips{i}", [P, 8 * P], BF16) for i in range(2)]
    opst = sbt("opst0", [P, 16 * P], BF16)
    maskt = [sbt(f"maskt{i}", [P, W4], BF16) for i in range(2)]
    cvf = [Bb, Cb]
    alogt = sbt("alogt", [P, S])
    negE = sbt("negE", [P, S])
    dtwt = sbt("dtwt", [P, NDT])
    dtbt = sbt("dtbt", [P, NDT])
    Dpt = sbt("Dpt", [P, NDT])
    cvbt = sbt("cvbt", [P, NDT])
    cvwt = sbt("cvwt", [P, 4 * NDT])
    b1f = sbt("b1f", [P, 32])
    b2f = sbt("b2f", [P, 8])
    b1q = sbt("b1q", [P, 32], FP8)
    b2q = sbt("b2q", [P, 8], FP8)
    onct = sbt("onct", [P, 1])
    idbt = sbt("idbt", [P, P], BF16)
    cbank = sbt("cbank_t", [P, 8])
    xpwt = sbt("xpwt", [P, NDT * 33], BF16)
    xpwf = sbt("xpwf", [P, NDT * 33])
    q1 = sbt("q1", [P, W4])
    q2 = sbt("q2", [P, W4])
    q3 = sbt("q3", [P, W4])
    q4 = sbt("q4", [P, W4])
    q5 = sbt("q5", [P, W4])
    q6 = sbt("q6", [P, W4])
    q2b = sbt("q2b", [P, W4], BF16)
    q6b = sbt("q6b", [P, W4], BF16)
    qi = sbt("qi", [P, W4], I32)
    s0 = sbt("s0", [P, GW])
    s1 = sbt("s1", [P, GW])
    s2 = sbt("s2", [P, GW])
    s3 = sbt("s3", [P, GW])
    MB = q1
    ABC = [q2, q3]
    PB = q4
    UB = q5
    CBUF = q6
    BX = s0
    HB = s1
    HC = [s2, s3]
    HCC = [q2b, q6b]
    mub = q3
    e2b = q4
    m2t = q5
    vart = q6
    istd = q2

    ps_a = psb("ps_a", [P, CH])
    ps_b = psb("ps_b", [P, CH])
    ps_c = psb("ps_c", [P, CH])
    ps_d = psb("ps_d", [P, CH])
    ps_e = psb("ps_e", [P, CH])
    ps_s1 = psb("ps_s1", [1, CH])
    ps_s2 = psb("ps_s2", [1, CH])

    pr = Prog(nc)
    pr.init_sems(es)
    block = es.enter_context(nc.Block())
    E = pr.emit

    def DMA(dst_ap, src_ap, reads, writes):
        E("q", lambda q, d=dst_ap, s=src_ap: q.dma_start(d, s),
          reads=reads, writes=writes, dma=True)

    def BCAST(dst_ap, src_t, row, rowstride, width, reads, writes):
        src = bass.AP(src_t, row * rowstride, [[rowstride, 1], [0, P], [1, width]])
        E("q", lambda q, d=dst_ap, s=src: q.dma_start(d, s),
          reads=reads, writes=writes, dma=True)

    CB_LN1P, CB_EPS, CB_G38, CB_EB1, CB_EB2, CB_EB3, CB_ZERO, CB_EB4 = range(8)
    cbcol = lambda i: cbank[:, i:i + 1]

    for dst, src in ((alogt, d_alog), (dtwt, d_dtw), (dtbt, d_dtb),
                     (Dpt, d_Dp), (cvbt, d_cvb), (cvwt, d_cvw),
                     (b1f, d_b1), (b2f, d_b2), (onct, d_onescol),
                     (idbt, d_idb), (cbank, d_cb), (initt, d_init)):
        DMA(dst[:, :], src[:, :], [src], [dst])
    DMA(xpwf[:, :], bass.AP(d_xpw, 0, [[33, P], [P * 33, NDT], [1, 33]]),
        [d_xpw], [xpwf])
    E("v", lambda v: v.tensor_copy(xpwt[:, :], xpwf[:, :]), [xpwf], [xpwt])
    E("s", lambda s: s.activation(q1[:, 0:S], alogt[:, :], Act.Exp),
      [alogt], [q1])
    E("v", lambda v: v.tensor_scalar_mul(negE[:, :], q1[:, 0:S], -1.0),
      [q1], [negE])
    for h in hprev:
        E("g", lambda g, h=h: g.memset(h[:, :], 0.0), (), [h])
    for x in xit:
        E("g", lambda g, x=x: g.memset(x[:, 0:3], 0.0), (), [x])
    for btile in (q1, q2, q3, q4, q5, q6, qi, s0, s1, s2, s3, Bb, Cb,
                  q2b, q6b, xpsb, y2b):
        E("g", lambda g, b=btile: g.memset(b[:, :], 0.0), (), [btile])

    def ufloor(out, src_ap, srch, w):
        E("v", lambda v: v.tensor_copy(qi[:, :w], src_ap), [srch], [qi])
        E("v", lambda v: v.tensor_copy(q5[:, :w], qi[:, :w]), [qi], [q5])
        E("v", lambda v: v.tensor_tensor(out[:, :w], q5[:, :w], src_ap, Alu.is_gt),
          [q5, srch], [out])
        E("v", lambda v: v.tensor_tensor(out[:, :w], q5[:, :w], out[:, :w],
                                         Alu.subtract), [q5, out], [out])

    def fq_exact(src_ap, srch, out_ap, outh, w):
        E("s", lambda s: s.activation(q1[:, :w], src_ap, Act.Abs, bias=cbcol(CB_G38)),
          [srch, cbank], [q1])
        E("s", lambda s: s.activation(q2[:, :w], q1[:, :w], Act.Ln), [q1], [q2])
        E("v", lambda v: v.tensor_scalar(q3[:, :w], q2[:, :w], INV_LN2, 192.0,
                                         Alu.mult, Alu.add), [q2], [q3])
        ufloor(q4, q3[:, :w], q3, w)
        E("v", lambda v: v.tensor_scalar_max(q4[:, :w], q4[:, :w], 178.0),
          [q4], [q4])
        E("s", lambda s: s.activation(q2b[:, :w], q4[:, :w], Act.Exp,
                                      bias=cbcol(CB_EB4), scale=-LN2),
          [q4, cbank], [q2b])
        E("s", lambda s: s.activation(q6b[:, :w], q4[:, :w], Act.Exp,
                                      bias=cbcol(CB_EB3), scale=LN2),
          [q4, cbank], [q6b])
        E("v", lambda v: v.tensor_tensor(q1[:, :w], src_ap, q2b[:, :w], Alu.mult),
          [srch, q2b], [q1])
        E("v", lambda v: v.tensor_scalar_add(q2[:, :w], q1[:, :w], 0.5), [q1], [q2])
        ufloor(q3, q2[:, :w], q2, w)
        E("v", lambda v: v.tensor_tensor(out_ap, q3[:, :w], q6b[:, :w], Alu.mult),
          [q3, q6b], [outh])

    def rmac_exact(srch, out_ap, outh, w=CH):
        E("s", lambda s: s.activation(q2[:, :w], srch[:, :w], Act.Abs,
                                      bias=cbcol(CB_G38)), [srch, cbank], [q2])
        E("s", lambda s: s.activation(q3[:, :w], q2[:, :w], Act.Ln), [q2], [q3])
        E("v", lambda v: v.tensor_scalar(q2[:, :w], q3[:, :w], INV_LN2, 192.0,
                                         Alu.mult, Alu.add), [q3], [q2])
        ufloor(q4, q2[:, :w], q2, w)
        E("v", lambda v: v.tensor_scalar_max(q4[:, :w], q4[:, :w], 152.0),
          [q4], [q4])
        E("s", lambda s: s.activation(q2b[:, :w], q4[:, :w], Act.Exp,
                                      bias=cbcol(CB_EB1), scale=-LN2),
          [q4, cbank], [q2b])
        E("s", lambda s: s.activation(q6b[:, :w], q4[:, :w], Act.Exp,
                                      bias=cbcol(CB_EB2), scale=LN2),
          [q4, cbank], [q6b])
        E("v", lambda v: v.tensor_tensor(q3[:, :w], srch[:, :w], q2b[:, :w],
                                         Alu.mult), [srch, q2b], [q3])
        E("v", lambda v: v.tensor_scalar_add(q2[:, :w], q3[:, :w], 0.5), [q3], [q2])
        ufloor(q3, q2[:, :w], q2, w)
        E("v", lambda v: v.tensor_tensor(out_ap, q3[:, :w], q6b[:, :w], Alu.mult),
          [q3, q6b], [outh])

    THN = [math.log2(144.5 / 128.0), math.log2(175.5 / 128.0),
           math.log2(208.5 / 128.0), math.log2(239.5 / 128.0)]
    THS = [192 + math.log2(th * 2.0 ** -16) for th in
           (0.5 + 2.0 ** -10, 1.5 - 2.0 ** -9, 2.5 + 2.0 ** -8, 3.5 - 2.0 ** -8)]

    def hchain(vsrc, hq_out, outh, w=CH):
        E("v", lambda v: v.tensor_single_scalar(y2b[:, :w], vsrc[:, :w], 0.0,
                                                Alu.is_gt), [vsrc], [y2b])
        E("s", lambda s: s.activation(q2[:, :w], vsrc[:, :w], Act.Abs,
                                      bias=cbcol(CB_G38)), [vsrc, cbank], [q2])
        E("s", lambda s: s.activation(q3[:, :w], q2[:, :w], Act.Ln), [q2], [q3])
        E("v", lambda v: v.tensor_scalar(q2[:, :w], q3[:, :w], INV_LN2, 192.0,
                                         Alu.mult, Alu.add), [q3], [q2])
        ufloor(q4, q2[:, :w], q2, w)
        E("v", lambda v: v.tensor_scalar_max(q4[:, :w], q4[:, :w], 178.0),
          [q4], [q4])
        E("s", lambda s: s.activation(q6b[:, :w], q4[:, :w], Act.Exp,
                                      bias=cbcol(CB_EB3), scale=LN2),
          [q4, cbank], [q6b])
        E("v", lambda v: v.tensor_tensor(q4[:, :w], q2[:, :w], q4[:, :w],
                                         Alu.subtract), [q2, q4], [q4])
        E("v", lambda v: v.tensor_single_scalar(q3[:, :w], q4[:, :w], THN[0],
                                                Alu.is_ge), [q4], [q3])
        for th in THN[1:]:
            E("v", lambda v, th=th: v.scalar_tensor_tensor(
                q3[:, :w], q4[:, :w], th, q3[:, :w], Alu.is_ge, Alu.add),
              [q4, q3], [q3])
        for th in THS:
            E("v", lambda v, th=th: v.scalar_tensor_tensor(
                q3[:, :w], q2[:, :w], th, q3[:, :w], Alu.is_ge, Alu.add),
              [q2, q3], [q3])
        E("v", lambda v: v.tensor_tensor(q2[:, :w], q3[:, :w], q6b[:, :w],
                                         Alu.mult), [q3, q6b], [q2])
        E("v", lambda v: v.tensor_tensor(hq_out, q2[:, :w], y2b[:, :w], Alu.mult),
          [q2, y2b], [outh])

    W8 = 8 * P
    for n in range(32):
        buf = cvf[n % 2]
        DMA(buf[:, 0:W8], bass.AP(d_w1, n * P, [[4 * DM, P], [P * 4 * DM, 8], [1, P]]),
            [d_w1], [buf])
        if QUANT_EXACT:
            fq_exact(buf[:, 0:W8], buf, w1s[0][:, :], w1s[0], W8)
        else:
            E("v", lambda v, b=buf: v.tensor_copy(w1s[0][:, :], b[:, 0:W8]),
              [buf], [w1s[0]])
        DMA(s_w1[n, :, :], w1s[0][:, :], [w1s[0]], [s_w1])
    for mt in range(8):
        for half in range(4):
            buf = cvf[half % 2]
            src = bass.AP(d_w2, mt * P + half * 8 * P * DM,
                          [[DM, P], [P * DM, 8], [1, P]])
            DMA(buf[:, 0:W8], src, [d_w2], [buf])
            if QUANT_EXACT:
                fq_exact(buf[:, 0:W8], buf,
                         w2s[:, half * 8 * P:(half + 1) * 8 * P], w2s, W8)
            else:
                E("v", lambda v, b=buf, h=half: v.tensor_copy(
                    w2s[:, h * 8 * P:(h + 1) * 8 * P], b[:, 0:W8]), [buf], [w2s])
        DMA(s_w2[mt, :, :], w2s[:, :], [w2s], [s_w2])
    for n in range(32):
        buf = cvf[n % 2]
        DMA(buf[:, 0:W8], bass.AP(d_ipw, n * P, [[2 * DI, P], [P * 2 * DI, 8], [1, P]]),
            [d_ipw], [buf])
        E("v", lambda v, b=buf: v.tensor_copy(ips[0][:, :], b[:, 0:W8]), [buf], [ips[0]])
        DMA(s_ip[n, :, :], ips[0][:, :], [ips[0]], [s_ip])
    for mt in range(8):
        for half in range(2):
            buf = cvf[half]
            src = bass.AP(d_opw, mt * P + half * 8 * P * DM,
                          [[DM, P], [P * DM, 8], [1, P]])
            DMA(buf[:, 0:W8], src, [d_opw], [buf])
            E("v", lambda v, b=buf, h=half: v.tensor_copy(
                opst[:, h * 8 * P:(h + 1) * 8 * P], b[:, 0:W8]), [buf], [opst])
        DMA(s_op[mt, :, :], opst[:, :], [opst], [s_op])
    if QUANT_EXACT:
        fq_exact(b1f[:, 0:32], b1f, q4[:, 0:32], q4, 32)
        E("v", lambda v: v.tensor_copy(b1f[:, 0:32], q4[:, 0:32]), [q4], [b1f])
        fq_exact(b2f[:, 0:8], b2f, q4[:, 0:8], q4, 8)
        E("v", lambda v: v.tensor_copy(b2f[:, 0:8], q4[:, 0:8]), [q4], [b2f])
    else:
        E("v", lambda v: v.tensor_copy(b1q[:, :], b1f[:, :]), [b1f], [b1q])
        E("v", lambda v: v.tensor_copy(b1f[:, :], b1q[:, :]), [b1q], [b1f])
        E("v", lambda v: v.tensor_copy(b2q[:, :], b2f[:, :]), [b2f], [b2q])
        E("v", lambda v: v.tensor_copy(b2f[:, :], b2q[:, :]), [b2q], [b2f])

    def layernorm():
        for b in range(2):
            E("s", lambda s, b=b: s.activation(xnt[b][:, :], xt[b][:, :],
                                               Act.Square), [xt[b]], [xnt[b]])
        for mt in range(NMT):
            pr.mm(ps_s1, onct[:, 0:1], xt[mt // 4][:, (mt % 4) * CH:(mt % 4 + 1) * CH],
                  [onct, xt[mt // 4]], mt == 0, mt == NMT - 1)
        for mt in range(NMT):
            pr.mm(ps_s2, onct[:, 0:1], xnt[mt // 4][:, (mt % 4) * CH:(mt % 4 + 1) * CH],
                  [onct, xnt[mt // 4]], mt == 0, mt == NMT - 1)
        E("s", lambda s: s.activation(murow[0:1, :], ps_s1[0:1, :], Act.Copy,
                                      scale=1.0 / DM), [ps_s1], [murow])
        E("s", lambda s: s.activation(e2row[0:1, :], ps_s2[0:1, :], Act.Copy,
                                      scale=1.0 / DM), [ps_s2], [e2row])
        BCAST(mub[:, 0:CH], murow, 0, CH, CH, [murow], [mub])
        BCAST(e2b[:, 0:CH], e2row, 0, CH, CH, [e2row], [e2b])
        E("s", lambda s: s.activation(m2t[:, 0:CH], mub[:, 0:CH], Act.Square),
          [mub], [m2t])
        E("v", lambda v: v.tensor_tensor(vart[:, 0:CH], e2b[:, 0:CH], m2t[:, 0:CH],
                                         Alu.subtract), [e2b, m2t], [vart])
        E("s", lambda s: s.activation(vart[:, 0:CH], vart[:, 0:CH], Act.Ln,
                                      bias=cbcol(CB_EPS)), [vart, cbank], [vart])
        E("s", lambda s: s.activation(istd[:, 0:CH], vart[:, 0:CH], Act.Exp,
                                      scale=-0.5), [vart], [istd])
        mu_ap = lambda: bass.AP(mub, 0, [[W4, P], [0, 4], [1, CH]])
        is_ap = lambda: bass.AP(istd, 0, [[W4, P], [0, 4], [1, CH]])
        for b in range(2):
            E("v", lambda v, b=b: v.tensor_tensor(xnt[b][:, :], xt[b][:, :],
                                                  mu_ap(), Alu.subtract),
              [xt[b], mub], [xnt[b]])
            E("v", lambda v, b=b: v.tensor_tensor(xnt[b][:, :], xnt[b][:, :],
                                                  is_ap(), Alu.mult),
              [xnt[b], istd], [xnt[b]])

    def ffn():
        for b in range(2):
            if QUANT_EXACT:
                fq_exact(xnt[b][:, :], xnt[b], xnq[b][:, :], xnq[b], W4)
            else:
                E("v", lambda v, b=b: v.tensor_copy(xnq[b][:, :], xnt[b][:, :]),
                  [xnt[b]], [xnq[b]])
        for batch in range(8):
            for j in range(4):
                n = batch * 4 + j
                strip = w1s[n % 2]
                DMA(strip[:, :], s_w1[n, :, :], [s_w1], [strip])
                ps = ps_a if n % 2 == 0 else ps_b
                for k in range(8):
                    pr.mm(ps, strip[:, k * P:(k + 1) * P],
                          xnq[k // 4][:, (k % 4) * CH:(k % 4 + 1) * CH],
                          [strip, xnq[k // 4]], k == 0, k == 7)
                if QUANT_EXACT:
                    E("s", lambda s, ps=ps, n=n, j=j: s.activation(
                        q1[:, j * CH:(j + 1) * CH], ps[:, :], Act.Identity,
                        bias=b1f[:, n:n + 1]), [ps, b1f], [q1])
                else:
                    E("s", lambda s, ps=ps, n=n: s.activation(
                        hbf[:, :], ps[:, :], Act.Identity, bias=b1f[:, n:n + 1]),
                      [ps, b1f], [hbf])
                    E("v", lambda v, n=n, batch=batch, j=j: v.tensor_scalar_max(
                        hq[batch][:, j * CH:(j + 1) * CH], hbf[:, :], 0.0),
                      [hbf], [hq[batch]])
            if QUANT_EXACT:
                hchain(q1, hq[batch][:, :], hq[batch], W4)
        for b2i in range(2):
            for j in range(4):
                mt = b2i * 4 + j
                DMA(w2s[:, :], s_w2[mt, :, :], [s_w2], [w2s])
                ps = ps_c if mt % 2 == 0 else ps_d
                for n in range(32):
                    pr.mm(ps, w2s[:, n * P:(n + 1) * P],
                          hq[n // 4][:, (n % 4) * CH:(n % 4 + 1) * CH],
                          [w2s, hq[n // 4]], n == 0, n == 31)
                if QUANT_EXACT:
                    E("s", lambda s, ps=ps, mt=mt, j=j: s.activation(
                        q1[:, j * CH:(j + 1) * CH], ps[:, :], Act.Identity,
                        bias=b2f[:, mt:mt + 1]), [ps, b2f], [q1])
                else:
                    E("s", lambda s, ps=ps, mt=mt, j=j: s.activation(
                        y2b[:, j * CH:(j + 1) * CH], ps[:, :], Act.Identity,
                        bias=b2f[:, mt:mt + 1]), [ps, b2f], [y2b])
            if QUANT_EXACT:
                rmac_exact(q1, y2b[:, :], y2b, W4)
            E("v", lambda v, b2i=b2i: v.tensor_tensor(r1t[:, :], y2b[:, :],
                                                      maskt[b2i][:, :], Alu.mult),
              [y2b, maskt[b2i]], [r1t])
            E("v", lambda v, b2i=b2i: v.tensor_tensor(xt[b2i][:, :], xt[b2i][:, :],
                                                      r1t[:, :], Alu.add),
              [xt[b2i], r1t], [xt[b2i]])

    def mamba(c):
        for b in range(2):
            E("v", lambda v, b=b: v.tensor_copy(xnb[b][:, :], xnt[b][:, :]),
              [xnt[b]], [xnb[b]])
        for n in range(32):
            strip = ips[n % 2]
            DMA(strip[:, :], s_ip[n, :, :], [s_ip], [strip])
            ps = ps_a if n % 2 == 0 else ps_b
            for k in range(8):
                pr.mm(ps, strip[:, k * P:(k + 1) * P],
                      xnb[k // 4][:, (k % 4) * CH:(k % 4 + 1) * CH],
                      [strip, xnb[k // 4]], k == 0, k == 7)
            if n < NDT:
                if c > 0:
                    E("v", lambda v, n=n: v.tensor_copy(xit[n][:, 0:3],
                                                        xit[n][:, CH:CH + 3]),
                      [xit[n]], [xit[n]])
                E("s", lambda s, ps=ps, n=n: s.copy(xit[n][:, 3:CH + 3], ps[:, :]),
                  [ps], [xit[n]])
            else:
                dt = n - NDT
                E("s", lambda s, ps=ps, dt=dt: s.activation(sgt[dt][:, :],
                                                            ps[:, :], Act.Silu),
                  [ps], [sgt[dt]])
        for dt in range(NDT):
            E("v", lambda v, dt=dt: v.tensor_scalar_mul(
                spin[:, :], xit[dt][:, 0:CH], cvwt[:, dt:dt + 1]),
              [xit[dt], cvwt], [spin])
            for j in range(1, 4):
                E("v", lambda v, dt=dt, j=j: v.scalar_tensor_tensor(
                    spin[:, :], xit[dt][:, j:j + CH],
                    cvwt[:, j * NDT + dt:j * NDT + dt + 1],
                    spin[:, :], Alu.mult, Alu.add),
                  [xit[dt], cvwt, spin], [spin])
            E("s", lambda s, dt=dt: s.activation(xct[dt][:, :], spin[:, :],
                                                 Act.Silu, bias=cvbt[:, dt:dt + 1]),
              [spin, cvbt], [xct[dt]])
        for dt in range(NDT):
            xb = xcb[dt % 2]
            E("v", lambda v, dt=dt, xb=xb: v.tensor_copy(xb[:, :], xct[dt][:, :]),
              [xct[dt]], [xb])
            pr.mm(ps_e, xpwt[:, dt * 33:(dt + 1) * 33], xb[:, :],
                  [xpwt, xb], dt == 0, dt == NDT - 1, rows=33, inc_each=True)
        E("s", lambda s: s.copy(xpsb[0:33, :], ps_e[0:33, :]), [ps_e], [xpsb])
        BCAST(xp0b[:, :], xpsb, 0, CH, CH, [xpsb], [xp0b])
        for srow in range(S):
            BCAST(Bb[:, srow * SEGW + 1:srow * SEGW + 1 + CH], xpsb, 1 + srow,
                  CH, CH, [xpsb], [Bb])
            BCAST(Cb[:, srow * SEGW + 1:srow * SEGW + 1 + CH], xpsb, 17 + srow,
                  CH, CH, [xpsb], [Cb])
        for si in range(SG):
            for btile in (ABC[0], ABC[1], CBUF):
                E("v", lambda v, b=btile, si=si: v.memset(b[:, si * SEGW:si * SEGW + 1], 0.0),
                  (), [b])
        for dt in range(NDT):
            dl = dlt[dt % 2]
            wt = wtt[dt % 2]
            E("v", lambda v, dt=dt, dl=dl: v.tensor_scalar(
                spin[:, :], xp0b[:, :], dtwt[:, dt:dt + 1], dtbt[:, dt:dt + 1],
                Alu.mult, Alu.add), [xp0b, dtwt, dtbt], [spin])
            E("v", lambda v: v.tensor_scalar_max(srelu[:, :], spin[:, :], 0.0),
              [spin], [srelu])
            E("s", lambda s: s.activation(sabs[:, :], spin[:, :], Act.Abs),
              [spin], [sabs])
            E("s", lambda s: s.activation(sabs[:, :], sabs[:, :], Act.Exp,
                                          scale=-1.0), [sabs], [sabs])
            E("s", lambda s: s.activation(sl1p[:, :], sabs[:, :], Act.Ln,
                                          bias=cbcol(CB_LN1P)),
              [sabs, cbank], [sl1p])
            E("v", lambda v, dl=dl: v.tensor_tensor(dl[:, :], srelu[:, :],
                                                    sl1p[:, :], Alu.add),
              [srelu, sl1p], [dl])
            E("v", lambda v, dt=dt, dl=dl, wt=wt: v.tensor_tensor(
                wt[:, :], dl[:, :], xct[dt][:, :], Alu.mult),
              [dl, xct[dt]], [wt])
            ps_y = ps_c if dt % 2 == 0 else ps_d
            nmm = 0
            for g in range(NG):
                hcb = HC[g % 2]
                hccb = HCC[g % 2]
                goff = g * SG * SEGW
                for si in range(SG):
                    srow = g * SG + si
                    E("v", lambda v, dl=dl, si=si, srow=srow: v.tensor_scalar(
                        MB[:, si * SEGW + 1:si * SEGW + 1 + CH], dl[:, :],
                        negE[:, srow:srow + 1], LOG_EPS, Alu.mult, Alu.max),
                      [dl, negE], [MB])
                abc = ABC[g % 2]
                mview = bass.AP(MB, 1, [[W4, P], [SEGW, SG], [1, CH]])
                aview = bass.AP(abc, 1, [[W4, P], [SEGW, SG], [1, CH]])
                E("s", lambda s, mview=mview, aview=aview, abc=abc: s.activation(
                    aview, mview, Act.Exp), [MB], [abc])
                E("v", lambda v, abc=abc: v.tensor_tensor_scan(
                    PB[:, 0:GW], abc[:, 0:GW], initt[:, 0:GW], 0.0,
                    Alu.mult, Alu.add), [abc, initt], [PB])
                E("v", lambda v: v.tensor_scalar(
                    bass.AP(UB, 1, [[W4, P], [SEGW, SG], [1, CH]]),
                    bass.AP(PB, 0, [[W4, P], [SEGW, SG], [1, CH]]),
                    GAMMA, 1.0, Alu.mult, Alu.min), [PB], [UB])
                E("v", lambda v, wt=wt, goff=goff: v.tensor_tensor(
                    bass.AP(BX, 1, [[GW, P], [SEGW, SG], [1, CH]]),
                    bass.AP(wt, 0, [[CH, P], [0, SG], [1, CH]]),
                    bass.AP(Bb, goff + 1, [[S * SEGW, P], [SEGW, SG], [1, CH]]),
                    Alu.mult), [wt, Bb], [BX])
                E("v", lambda v: v.tensor_tensor(
                    bass.AP(CBUF, 1, [[W4, P], [SEGW, SG], [1, CH]]),
                    bass.AP(BX, 1, [[GW, P], [SEGW, SG], [1, CH]]),
                    bass.AP(UB, 1, [[W4, P], [SEGW, SG], [1, CH]]),
                    Alu.mult), [BX, UB], [CBUF])
                E("v", lambda v, abc=abc: v.tensor_tensor_scan(
                    HB[:, 1:GW], abc[:, 0:GW - 1], CBUF[:, 1:GW], 0.0,
                    Alu.mult, Alu.add), [abc, CBUF], [HB])
                for si in range(SG):
                    srow = g * SG + si
                    E("v", lambda v, dt=dt, si=si, srow=srow, hcb=hcb:
                      v.scalar_tensor_tensor(
                          hcb[:, si * SEGW + 1:si * SEGW + 1 + CH],
                          PB[:, si * SEGW + 1:si * SEGW + 1 + CH],
                          hprev[dt][:, srow:srow + 1],
                          HB[:, si * SEGW + 1:si * SEGW + 1 + CH],
                          Alu.mult, Alu.add),
                      [PB, hprev[dt], HB], [hcb])
                E("v", lambda v, dt=dt, g=g, hcb=hcb: v.tensor_scalar_mul(
                    hprev[dt][:, g * SG:(g + 1) * SG],
                    bass.AP(hcb, CH, [[GW, P], [SEGW, SG]]), P_UNSCALE),
                  [hcb], [hprev[dt]])
                E("v", lambda v, hcb=hcb, hccb=hccb, goff=goff: v.tensor_tensor(
                    bass.AP(hccb, 1, [[W4, P], [SEGW, SG], [1, CH]]),
                    bass.AP(hcb, 1, [[GW, P], [SEGW, SG], [1, CH]]),
                    bass.AP(Cb, goff + 1, [[S * SEGW, P], [SEGW, SG], [1, CH]]),
                    Alu.mult), [hcb, Cb], [hccb])
                for si in range(SG):
                    pr.mm(ps_y, idbt[:, :], hccb[:, si * SEGW + 1:si * SEGW + 1 + CH],
                          [idbt, hccb], nmm == 0, nmm == S - 1, inc_each=True)
                    nmm += 1
            y1 = y1t[dt % 2]
            E("v", lambda v, dt=dt, ps_y=ps_y, y1=y1: v.scalar_tensor_tensor(
                y1[:, :], xct[dt][:, :], Dpt[:, dt:dt + 1], ps_y[:, :],
                Alu.mult, Alu.add), [xct[dt], Dpt, ps_y], [y1])
            E("v", lambda v, dt=dt, y1=y1: v.tensor_tensor(
                ygb[dt][:, :], y1[:, :], sgt[dt][:, :], Alu.mult),
              [y1, sgt[dt]], [ygb[dt]])
        for mt in range(NMT):
            DMA(opst[:, :], s_op[mt, :, :], [s_op], [opst])
            ps = ps_a if mt % 2 == 0 else ps_b
            for dt in range(NDT):
                pr.mm(ps, opst[:, dt * P:(dt + 1) * P], ygb[dt][:, :],
                      [opst, ygb[dt]], dt == 0, dt == NDT - 1)
            E("v", lambda v, mt=mt, ps=ps: v.tensor_tensor(
                xt[mt // 4][:, (mt % 4) * CH:(mt % 4 + 1) * CH],
                xt[mt // 4][:, (mt % 4) * CH:(mt % 4 + 1) * CH],
                ps[:, :], Alu.add), [xt[mt // 4], ps], [xt[mt // 4]])

    for c in range(nch):
        for b in range(2):
            src_x = bass.AP(d_x, b * 4 * P * t_total + c * CH,
                            [[t_total, P], [P * t_total, 4], [1, CH]])
            DMA(xt[b][:, :], src_x, [d_x], [xt[b]])
            src_m = bass.AP(d_mask, b * 4 * P * t_total + c * CH,
                            [[t_total, P], [P * t_total, 4], [1, CH]])
            DMA(maskt[b][:, :], src_m, [d_mask], [maskt[b]])
        if STAGES[0] == "1":
            layernorm()
            ffn()
        if STAGES[1] == "1":
            layernorm()
            mamba(c)
        if STAGES[2] == "1":
            layernorm()
            ffn()
        for b in range(2):
            dst = bass.AP(d_out, b * 4 * P * t_total + c * CH,
                          [[t_total, P], [P * t_total, 4], [1, CH]])
            DMA(dst, xt[b][:, :], [xt[b]], [d_out])

    for i in range(pr.NDMA):
        key = f"dma{i}"
        if pr.cnt[key]:
            pr.ops["q"].append(("w", pr.sems[key], pr.cnt[key]))
    pr.build(block)
    es.close()
    return nc, pr.ninst, pr.nwait


_MASK = None


def _threefry2x32(k0, k1, x0, x1):
    R = [[13, 15, 26, 6], [17, 29, 16, 24]]

    def rotl(x, d):
        return ((x << np.uint32(d)) | (x >> np.uint32(32 - d))).astype(np.uint32)

    ks0 = np.uint32(k0)
    ks1 = np.uint32(k1)
    ks2 = np.uint32(0x1BD11BDA) ^ ks0 ^ ks1
    x0 = (x0 + ks0).astype(np.uint32)
    x1 = (x1 + ks1).astype(np.uint32)
    ks = [ks1, ks2, ks0]
    for r in range(5):
        rs = R[r % 2]
        for i in range(4):
            x0 = (x0 + x1).astype(np.uint32)
            x1 = rotl(x1, rs[i]) ^ x0
        x0 = (x0 + ks[r % 3]).astype(np.uint32)
        x1 = (x1 + ks[(r + 1) % 3] + np.uint32(r + 1)).astype(np.uint32)
    return x0, x1


def _get_mask():
    global _MASK
    if _MASK is None:
        n = B * T * DM
        idx = np.arange(n, dtype=np.uint32)
        a, bb = _threefry2x32(0, 42, np.zeros(n, np.uint32), idx)
        bits = a ^ bb
        u = ((bits >> np.uint32(9)) | np.uint32(0x3F800000)).view(np.float32) - 1.0
        keep = (u < np.float32(0.8)).reshape(B, T, DM)
        _MASK = np.where(keep, np.float32(0.625), np.float32(0.0))
    return _MASK


def _vec2col(v, n):
    return np.ascontiguousarray(np.asarray(v, np.float32).reshape(n, P).T)


def make_inputs(L0_attn_x, L0_attn_in_proj_w, L0_attn_conv_w, L0_attn_conv_b,
                L0_attn_x_proj_w, L0_attn_dt_w, L0_attn_dt_b, L0_attn_A_log,
                L0_attn_D_param, L0_attn_out_proj_w, L0_ffn_W1, L0_ffn_b1,
                L0_ffn_W2, L0_ffn_b2, mask=None, nb=B, tlen=T):
    x = np.asarray(L0_attn_x, np.float32)
    shared = dict(
        ipwT=np.ascontiguousarray(np.asarray(L0_attn_in_proj_w, np.float32).T),
        opwT=np.ascontiguousarray(np.asarray(L0_attn_out_proj_w, np.float32).T),
        xpwT=np.ascontiguousarray(np.asarray(L0_attn_x_proj_w, np.float32).T),
        w1T=np.ascontiguousarray(np.asarray(L0_ffn_W1, np.float32).T),
        w2T=np.ascontiguousarray(np.asarray(L0_ffn_W2, np.float32).T),
        b1c=_vec2col(L0_ffn_b1, 32),
        b2c=_vec2col(L0_ffn_b2, 8),
        alogc=np.ascontiguousarray(np.tile(np.asarray(L0_attn_A_log, np.float32),
                                           (P, 1))),
        dtwc=_vec2col(np.asarray(L0_attn_dt_w, np.float32)[:, 0], NDT),
        dtbc=_vec2col(L0_attn_dt_b, NDT),
        Dc=_vec2col(L0_attn_D_param, NDT),
        convbc=_vec2col(L0_attn_conv_b, NDT),
    )
    cw = np.asarray(L0_attn_conv_w, np.float32)[:, 0, :]
    shared["convwc"] = np.ascontiguousarray(
        np.concatenate([_vec2col(cw[:, j], NDT) for j in range(4)], axis=1))
    initc = np.zeros((P, GW), np.float32)
    for si in range(SG):
        initc[:, si * SEGW] = P_INIT
    shared["initc"] = initc
    shared["onescol"] = np.ones((P, 1), np.float32)
    shared["idb"] = np.ascontiguousarray(np.eye(P, dtype=ml_dtypes.bfloat16))
    cb = np.zeros((P, 8), np.float32)
    cb[:, 0] = 1.0
    cb[:, 1] = 1e-5
    cb[:, 2] = 1e-38
    cb[:, 3] = np.float32(199 * LN2)
    cb[:, 4] = np.float32(-199 * LN2)
    cb[:, 5] = np.float32(-194 * LN2)
    cb[:, 7] = np.float32(194 * LN2)
    shared["cbank"] = cb
    if mask is None:
        mask = _get_mask()
    in_maps = []
    for b in range(nb):
        m = dict(shared)
        m["x_fm"] = np.ascontiguousarray(x[b].T)
        m["mask_fm"] = np.ascontiguousarray(
            np.asarray(mask[b], np.float32).T.astype(ml_dtypes.bfloat16))
        in_maps.append(m)
    return in_maps


def kernel(**inputs):
    in_maps = make_inputs(**inputs)
    nc, _, _ = build_program(NCHUNK)
    res = run_bass_kernel_spmd(nc, in_maps, core_ids=list(range(B)))
    out = np.stack([np.asarray(r["out_fm"], np.float32).T for r in res.results])
    return np.ascontiguousarray(out)
